# version 07
# baseline (speedup 1.0000x reference)
import contextlib
import numpy as np

import concourse.bass as bass
import concourse.mybir as mybir
from concourse.bass_utils import run_bass_kernel_spmd
from concourse.alu_op_type import AluOpType

F32 = mybir.dt.float32
BF16 = mybir.dt.bfloat16
AF = mybir.ActivationFunctionType
AX = mybir.AxisListType
OP = AluOpType

NCORE = 8
T = 1024
NCH = 8
D = 2048
KC = 16
H = 16
I = 8192
IT = 64
RW = 2
NSLOT = 8
EPS = 1e-5

_DEBUG_TAPS = ()


class Sched:
    ENGS = ("tensor", "vector", "scalar", "sync", "gpsimd")

    def __init__(self, nc, stack):
        self.nc = nc
        self.stack = stack
        self.ops = {e: [] for e in self.ENGS}
        self.cnt = {}
        self.h = {}
        self.lastwait = {e: {} for e in self.ENGS}
        self.dead = False

    def sem(self, name):
        self.h[name] = self.stack.enter_context(self.nc.semaphore(name))
        self.cnt[name] = 0
        return name

    def wait(self, eng, sem, val):
        if self.dead or val is None or val <= 0 or self.lastwait[eng].get(sem, 0) >= val:
            return
        self.lastwait[eng][sem] = val
        h = self.h[sem]
        self.ops[eng].append(("w", lambda e, h=h, v=val: e.wait_ge(h, v), None))

    def run(self, eng, fn, inc=None, n=1):
        if self.dead:
            return None
        if inc is None:
            self.ops[eng].append(("i", fn, fn))
            return None
        self.cnt[inc] += n
        h = self.h[inc]
        self.ops[eng].append(
            ("I", lambda e, fn=fn, h=h, n=n: fn(e).then_inc(h, n), None))
        return self.cnt[inc]

    def mark(self, eng, sem):
        if self.dead:
            return None
        lst = self.ops[eng]
        for idx in range(len(lst) - 1, -1, -1):
            kind, _, raw = lst[idx]
            if kind == "w":
                continue
            if kind == "i":
                self.cnt[sem] += 1
                h = self.h[sem]
                lst[idx] = ("I", lambda e, raw=raw, h=h: raw(e).then_inc(h, 1), None)
                return self.cnt[sem]
            break
        self.cnt[sem] += 1
        h = self.h[sem]
        lst.append(("i", lambda e: e.drain(), None))
        lst.append(("I", lambda e, h=h: e.nop().then_inc(h, 1), None))
        return self.cnt[sem]

    def dma(self, eng, out, in_, sem):
        return self.run(eng, lambda e, o=out, i=in_: e.dma_start(out=o, in_=i),
                        inc=sem, n=16)


def build_program(taps=()):
    import os as _os
    _phase_stop = _os.environ.get("PHASE_STOP")

    def maybe_stop(tag):
        if _phase_stop == tag:
            S.dead = True

    nc = bass.Bass("TRN2", target_bir_lowering=False)
    stack = contextlib.ExitStack()

    x_in = nc.dram_tensor("x", [T, D], F32, kind="ExternalInput").ap()
    wq_in = nc.dram_tensor("wq", [D, D], F32, kind="ExternalInput")
    wk_in = nc.dram_tensor("wk", [D, D], F32, kind="ExternalInput")
    wv_in = nc.dram_tensor("wv", [D, D], F32, kind="ExternalInput")
    wo_in = nc.dram_tensor("wo", [D, D], F32, kind="ExternalInput")
    wg1_in = nc.dram_tensor("wg1", [128, D], F32, kind="ExternalInput")
    wg2_in = nc.dram_tensor("wg2", [128, D], F32, kind="ExternalInput")
    ld_in = nc.dram_tensor("ldec", [128, H], F32, kind="ExternalInput").ap()
    nw_in = nc.dram_tensor("nw", [128, H], F32, kind="ExternalInput").ap()
    ck_in = nc.dram_tensor("ck", [128, NCH], F32, kind="ExternalInput").ap()
    consts_in = nc.dram_tensor("consts", [128, 386], F32, kind="ExternalInput")
    fg_in = nc.dram_tensor("fg", [I, D], F32, kind="ExternalInput")
    fu_in = nc.dram_tensor("fu", [I, D], F32, kind="ExternalInput")
    fd_in = nc.dram_tensor("fd", [I, D], F32, kind="ExternalInput")
    out_t = nc.dram_tensor("out", [T, D], F32, kind="ExternalOutput")

    g_bounce_in = nc.dram_tensor("g_bounce_in", [128, H * 128], F32)
    g_bounce_out = nc.dram_tensor("g_bounce_out", [NCORE * 128, H * 128], F32)

    tap_dram = {}
    for tname, shape in taps:
        tap_dram[tname] = nc.dram_tensor("tap_" + tname, shape, F32,
                                         kind="ExternalOutput").ap()

    S = Sched(nc, stack)

    def sb_open(name, shape, dtp, side="left"):
        g = nc.sbuf_tensor(name, shape, dtp, side=side)
        return g, g.__enter__()

    def ps_open(name, shape, dtp):
        g = nc.psum_tensor(name, shape, dtp)
        return g, g.__enter__()

    t_ev = S.sem("t_ev"); v_ev = S.sem("v_ev"); s_ev = S.sem("s_ev"); g_ev = S.sem("g_ev")
    sx = [S.sem(f"sx{i}") for i in range(NCH)]
    swp = [S.sem(f"swp{i}") for i in range(8)]
    sring = [S.sem(f"sr{i}") for i in range(RW)]
    swg = S.sem("swg")
    sconst = S.sem("sconst")
    sfg = [S.sem(f"sfg{i}") for i in range(RW)]
    sfu = [S.sem(f"sfu{i}") for i in range(RW)]
    sfd = [S.sem(f"sfd{i}") for i in range(6)]
    scc = S.sem("scc")
    sgb = S.sem("sgb")
    sga = [S.sem(f"sga{i}") for i in range(4)]
    sst = S.sem("sst")
    stap = S.sem("stap")

    def tap(name, ap_sbuf):
        if name not in tap_dram:
            return
        saved = S.dead
        S.dead = False
        S.wait("gpsimd", v_ev, S.cnt[v_ev])
        S.wait("gpsimd", s_ev, S.cnt[s_ev])
        S.wait("gpsimd", t_ev, S.cnt[t_ev])
        S.dma("gpsimd", tap_dram[name], ap_sbuf, stap)
        S.dead = saved

    def sbL(name, shape, dtp):
        return stack.enter_context(nc.sbuf_tensor(name, shape, dtp, side="left"))

    ld_b = sbL("ld_b", [128, H], F32)
    ck_b = sbL("ck_b", [128, NCH], F32)
    nw_sb = sbL("nw_sb", [128, H], F32)
    consts_sb = sbL("consts_sb", [128, 258], F32)
    ident = sbL("ident", [128, 128], BF16)
    posc = consts_sb[:, 0:1]
    poss = consts_sb[:, 1:2]
    diffM = consts_sb[:, 2:130]
    tri = consts_sb[:, 130:258]
    qdecT = sbL("qdecT", [128, H], F32)
    kdecT = sbL("kdecT", [128, H], F32)
    gchunk = sbL("gchunk", [128, H], F32)
    wcE = sbL("wcE", [128, NCH * H], F32)
    stats = sbL("stats", [128, 96], F32)

    SUM0, SQ0, RS0, NB0, VAR0, SD0, TMP0 = 16, 24, 32, 40, 48, 56, 64

    gPP0, pp0 = ps_open("pp0", [128, 512], F32)
    gPP1, pp1 = ps_open("pp1", [128, 512], F32)
    pp = [pp0, pp1]
    gPT2, pT2 = ps_open("pT2", [128, D], BF16)
    gPA, pA = ps_open("pA", [128, D], F32)

    maybe_stop("pnull")
    S.dma("sync", ld_b[:], ld_in, sconst)
    S.dma("sync", ck_b[:], ck_in, sconst)
    S.dma("sync", nw_sb[:], nw_in, sconst)
    S.dma("sync", consts_sb[:], consts_in[:, 0:258], sconst)
    S.dma("gpsimd", ident[:], consts_in[:, 258:386], swg)
    S.wait("scalar", sconst, 64)
    S.run("scalar", lambda e: e.activation(qdecT[:], ld_b[:], AF.Exp, scale=posc[:]))
    S.run("scalar", lambda e: e.activation(kdecT[:], ld_b[:], AF.Exp, scale=poss[:]))
    S.run("scalar", lambda e: e.activation(gchunk[:], ld_b[:], AF.Exp, scale=128.0), inc=s_ev)
    for j in range(NCH):
        S.run("scalar", lambda e, j=j: e.activation(
            wcE[:, j * H:(j + 1) * H], ld_b[:], AF.Exp, scale=ck_b[:, j:j + 1]))

    maybe_stop("p0")
    if "dec" in tap_dram:
        tap("dec", qdecT[:])
    if "kdec" in tap_dram:
        tap("kdec", kdecT[:])
    if "gch" in tap_dram:
        tap("gch", gchunk[:])
    if "wcE" in tap_dram:
        tap("wcE", wcE[:])
    maybe_stop("p0tap")
    gWG2, wg2_sb = sb_open("wg2_sb", [128, D], BF16, side="right")
    gXT, xhatT = sb_open("xhatT", [128, KC * T], BF16, side="right")
    gWG1, wg1_sb = sb_open("wg1_sb", [128, D], BF16, side="right")
    S.dma("gpsimd", wg1_sb[:], wg1_in[:, :], swg)
    S.dma("gpsimd", wg2_sb[:], wg2_in[:, :], swg)

    gWv, wv_sb = sb_open("wv_sb", [128, KC * D], BF16, side="right")
    for kc in range(KC):
        S.dma("gpsimd", wv_sb[:, kc * D:(kc + 1) * D],
              wv_in[kc * 128:(kc + 1) * 128, :], swp[kc // 2])
    gXB, xbuf = zip(*[sb_open(f"xb{i}", [128, D], F32, side="right") for i in range(2)])
    gHTM, xhtm = zip(*[sb_open(f"xhtm{i}", [128, D], BF16, side="right") for i in range(2)])
    gSCR, scr2k = sb_open("scr2k", [128, D], F32, side="right")

    def kcslice(xT, c):
        return xT[:, :].rearrange("p (kc t) -> p kc t", kc=KC)[:, :, bass.ts(c, 128)]

    pt2_3d = pT2[:].rearrange("p (kc t) -> p kc t", kc=KC)

    ln_prev_reduce = [0]

    def vstep(fn):
        ev = S.run("vector", fn, inc=v_ev)
        S.wait("vector", v_ev, ev)
        return ev

    def ln_stats(c, src_ap, scr):
        S.wait("scalar", v_ev, ln_prev_reduce[0])
        sq_ev = S.run("scalar", lambda e: e.activation(scr[:], src_ap, AF.Square),
                      inc=s_ev)
        vstep(lambda e: e.reduce_sum(stats[:, SUM0 + c:SUM0 + c + 1], src_ap,
                                     axis=AX.X))
        S.wait("vector", s_ev, sq_ev)
        ln_prev_reduce[0] = vstep(lambda e: e.reduce_sum(
            stats[:, SQ0 + c:SQ0 + c + 1], scr[:], axis=AX.X))
        vstep(lambda e: e.tensor_scalar(
            stats[:, SUM0 + c:SUM0 + c + 1], stats[:, SUM0 + c:SUM0 + c + 1],
            1.0 / D, None, OP.mult))
        vstep(lambda e: e.tensor_tensor(
            stats[:, RS0 + c:RS0 + c + 1], stats[:, SUM0 + c:SUM0 + c + 1],
            stats[:, SUM0 + c:SUM0 + c + 1], OP.mult))
        vstep(lambda e: e.tensor_scalar(
            stats[:, SQ0 + c:SQ0 + c + 1], stats[:, SQ0 + c:SQ0 + c + 1],
            1.0 / D, EPS, OP.mult, OP.add))
        vv = vstep(lambda e: e.scalar_tensor_tensor(
            out=stats[:, VAR0 + c:VAR0 + c + 1], in0=stats[:, RS0 + c:RS0 + c + 1],
            scalar=-1.0, in1=stats[:, SQ0 + c:SQ0 + c + 1], op0=OP.mult, op1=OP.add))
        S.wait("scalar", v_ev, vv)
        sv = S.run("scalar", lambda e: e.activation(
            stats[:, SD0 + c:SD0 + c + 1], stats[:, VAR0 + c:VAR0 + c + 1], AF.Sqrt),
            inc=s_ev)
        S.wait("vector", s_ev, sv)
        vstep(lambda e: e.reciprocal(stats[:, RS0 + c:RS0 + c + 1],
                                     stats[:, SD0 + c:SD0 + c + 1]))
        vstep(lambda e: e.tensor_tensor(
            stats[:, NB0 + c:NB0 + c + 1], stats[:, SUM0 + c:SUM0 + c + 1],
            stats[:, RS0 + c:RS0 + c + 1], OP.mult))
        return vstep(lambda e: e.tensor_scalar(
            stats[:, NB0 + c:NB0 + c + 1], stats[:, NB0 + c:NB0 + c + 1],
            -1.0, None, OP.mult))

    maybe_stop("p0c")
    ln1_norm_ev = [0] * NCH
    xT_ev = [0] * NCH
    trans_ev = [0] * NCH

    for c in range(2):
        S.dma("sync", xbuf[c][:], x_in[c * 128:(c + 1) * 128, :], sx[c])
    _p1cut = _os.environ.get("P1CUT")
    for c in range(NCH):
        S.wait("vector", sx[c], 16)
        S.wait("scalar", sx[c], 16)
        if _p1cut == "dma":
            S.dead = True
        stat_ev = ln_stats(c, xbuf[c % 2][:], scr2k)
        if _p1cut == "stats":
            S.dead = True
        S.wait("scalar", v_ev, stat_ev)
        if c >= 2:
            S.wait("scalar", t_ev, trans_ev[c - 2])
        ln1_norm_ev[c] = S.run("scalar", lambda e, c=c: e.activation(
            xhtm[c % 2][:], xbuf[c % 2][:], AF.Identity,
            scale=stats[:, RS0 + c:RS0 + c + 1], bias=stats[:, NB0 + c:NB0 + c + 1]),
            inc=s_ev)
        if _p1cut == "norm":
            S.dead = True
        S.wait("tensor", swg, 16)
        S.wait("tensor", s_ev, ln1_norm_ev[c])
        if c >= 1:
            S.wait("tensor", v_ev, xT_ev[c - 1])
        for ft in range(KC):
            S.run("tensor", lambda e, ft=ft, c=c: e.transpose(
                pT2[:, ft * 128:(ft + 1) * 128], xhtm[c % 2][:, ft * 128:(ft + 1) * 128],
                ident[:]))
        trans_ev[c] = S.mark("tensor", t_ev)
        if _p1cut == "trans":
            S.dead = True
        S.wait("vector", t_ev, trans_ev[c])
        if _os.environ.get("SIMPLE_COPY"):
            xT_ev[c] = S.run("vector", lambda e, c=c: e.tensor_copy(
                xhatT[:, c * D:(c + 1) * D], pT2[:]), inc=v_ev)
        else:
            xT_ev[c] = S.run("vector", lambda e, c=c: e.tensor_copy(
                kcslice(xhatT, c), pt2_3d), inc=v_ev)
        if c + 2 < NCH:
            S.wait("sync", s_ev, ln1_norm_ev[c])
            S.dma("sync", xbuf[(c + 2) % 2][:], x_in[(c + 2) * 128:(c + 3) * 128, :],
                  sx[c + 2])
        if _os.environ.get("P1N") and c + 1 >= int(_os.environ["P1N"]):
            S.dead = True

    maybe_stop("p1")
    if "xhatT" in tap_dram:
        tap("xhatT", xhatT[:])
    if "stats" in tap_dram:
        tap("stats", stats[:])

    gSCR.__exit__(None, None, None)
    for g in reversed(gHTM):
        g.__exit__(None, None, None)
    for g in reversed(gXB):
        g.__exit__(None, None, None)

    gV, v_sb = sb_open("v_sb", [128, NCH * D], BF16, side="left")

    pp_free_ev = [None, None]
    for tc in range(NCH):
        S.wait("tensor", v_ev, xT_ev[tc])
        for fb in range(4):
            pi = (tc * 4 + fb) % 2
            if pp_free_ev[pi] is not None:
                S.wait("tensor", s_ev, pp_free_ev[pi])
            for kc in range(KC):
                S.wait("tensor", swp[kc // 2], 16 * (kc % 2 + 1))
                S.run("tensor", lambda e, pi=pi, kc=kc, tc=tc, fb=fb: e.matmul(
                    pp[pi][:], xhatT[:, kc * T + tc * 128:kc * T + (tc + 1) * 128],
                    wv_sb[:, kc * D + fb * 512:kc * D + (fb + 1) * 512],
                    start=(kc == 0), stop=(kc == KC - 1)))
            tt = S.mark("tensor", t_ev)
            S.wait("scalar", t_ev, tt)
            pp_free_ev[pi] = S.run("scalar", lambda e, pi=pi, tc=tc, fb=fb: e.activation(
                v_sb[:, tc * D + fb * 512:tc * D + (fb + 1) * 512], pp[pi][:], AF.Silu),
                inc=s_ev)

    maybe_stop("p2a")
    v_last_mm = S.cnt[t_ev]
    gWv.__exit__(None, None, None)
    gKT, kT = sb_open("kT", [128, H * T], BF16, side="left")
    gKtil, ktil = sb_open("ktil", [128, H * 128], BF16, side="left")
    gRing, wring = sb_open("wring", [128, RW * D], BF16, side="right")

    ring_cons_ev = {}

    def ring_fill(slab, src):
        slot = slab % RW
        if slab >= RW:
            S.wait("gpsimd", t_ev, ring_cons_ev[slab - RW])
        else:
            S.wait("gpsimd", t_ev, v_last_mm)
        S.dma("gpsimd", wring[:, slot * D:(slot + 1) * D], src, sring[slot])

    def proj_fmajor(dst, w_dram, slab_base, n_ft, act, dst_stride=T):
        for ft in range(min(RW, n_ft)):
            ring_fill(slab_base + ft, w_dram[ft * 128:(ft + 1) * 128, :])
        for ft in range(n_ft):
            slab = slab_base + ft
            slot = slab % RW
            S.wait("tensor", sring[slot], 16 * (slab // RW + 1))
            for half in range(2):
                S.wait("tensor", v_ev, xT_ev[4 * half + 3])
                pi = (ft * 2 + half) % 2
                if pp_free_ev[pi] is not None:
                    S.wait("tensor", s_ev, pp_free_ev[pi])
                for kc in range(KC):
                    S.run("tensor", lambda e, pi=pi, slot=slot, kc=kc, half=half: e.matmul(
                        pp[pi][:], wring[:, slot * D + kc * 128:slot * D + (kc + 1) * 128],
                        xhatT[:, kc * T + half * 512:kc * T + (half + 1) * 512],
                        start=(kc == 0), stop=(kc == KC - 1)))
                tt = S.mark("tensor", t_ev)
                if half == 1:
                    ring_cons_ev[slab] = tt
                S.wait("scalar", t_ev, tt)
                pp_free_ev[pi] = S.run("scalar", lambda e, pi=pi, ft=ft, half=half,
                                       dst=dst, act=act, ds=dst_stride: e.activation(
                    dst[:, ft * ds + half * 512:ft * ds + (half + 1) * 512], pp[pi][:],
                    act), inc=s_ev)
            if slab_base + ft + RW < slab_base + n_ft:
                ring_fill(slab_base + ft + RW,
                          w_dram[(ft + RW) * 128:(ft + RW + 1) * 128, :])

    proj_fmajor(kT, wk_in, 0, KC, AF.Silu)
    k_done_ev = S.cnt[s_ev]
    if "v_sb" in tap_dram:
        tap("v_sb", v_sb[:])
    if "kT" in tap_dram:
        tap("kT", kT[:])

    maybe_stop("p2b")
    gGsb, Gsb = sb_open("G", [128, H * 128], F32, side="left")
    S.run("vector", lambda e: e.memset(Gsb[:], 0.0))
    ag_prev_evac = [None]

    def emit_AG_chunk(c, dst):
        S.wait("tensor", s_ev, k_done_ev)
        S.wait("tensor", v_ev, ag_prev_evac[0])
        for h in range(H):
            S.run("tensor", lambda e, h=h, c=c: e.transpose(
                pT2[:, h * 128:(h + 1) * 128],
                kT[:, h * T + c * 128:h * T + (c + 1) * 128], ident[:]))
        tt = S.mark("tensor", t_ev)
        S.wait("vector", t_ev, tt)
        for h in range(H):
            S.run("vector", lambda e, h=h: e.tensor_scalar(
                ktil[:, h * 128:(h + 1) * 128], pT2[:, h * 128:(h + 1) * 128],
                kdecT[:, h:h + 1], None, OP.mult))
        vv = S.mark("vector", v_ev)
        if c == 0 and "ktil" in tap_dram:
            tap("ktil", ktil[:])
        ag_prev_evac[0] = vv
        S.wait("tensor", v_ev, vv)
        for h in range(H):
            S.run("tensor", lambda e, h=h, c=c: e.matmul(
                pA[:, h * 128:(h + 1) * 128], ktil[:, h * 128:(h + 1) * 128],
                v_sb[:, c * D + h * 128:c * D + (h + 1) * 128], start=True, stop=True))
        tt = S.mark("tensor", t_ev)
        S.wait("vector", t_ev, tt)
        for h in range(H):
            S.run("vector", lambda e, h=h, dst=dst: e.scalar_tensor_tensor(
                out=dst[:, h * 128:(h + 1) * 128], in0=dst[:, h * 128:(h + 1) * 128],
                scalar=gchunk[:, h:h + 1], in1=pA[:, h * 128:(h + 1) * 128],
                op0=OP.mult, op1=OP.add))
        return S.mark("vector", v_ev)

    for c in range(NCH):
        g_ready = emit_AG_chunk(c, Gsb)

    if "Gsb" in tap_dram:
        tap("Gsb", Gsb[:])
    S.wait("sync", v_ev, g_ready)
    S.dma("sync", g_bounce_in.ap(), Gsb[:], sgb)

    maybe_stop("p25")
    gQT, qT = sb_open("qT", [128, H * T], BF16, side="left")
    gG1, g1T = sb_open("g1T", [128, T], BF16, side="left")

    proj_fmajor(qT, wq_in, KC, KC, AF.Silu)
    S.wait("tensor", swg, 48)
    for half in range(2):
        pi = half % 2
        if pp_free_ev[pi] is not None:
            S.wait("tensor", s_ev, pp_free_ev[pi])
        for kc in range(KC):
            S.run("tensor", lambda e, pi=pi, kc=kc, half=half: e.matmul(
                pp[pi][:], wg1_sb[:, kc * 128:(kc + 1) * 128],
                xhatT[:, kc * T + half * 512:kc * T + (half + 1) * 512],
                start=(kc == 0), stop=(kc == KC - 1)))
        tt = S.mark("tensor", t_ev)
        S.wait("scalar", t_ev, tt)
        pp_free_ev[pi] = S.run("scalar", lambda e, pi=pi, half=half: e.activation(
            g1T[:, half * 512:(half + 1) * 512], pp[pi][:], AF.Identity), inc=s_ev)

    maybe_stop("p2c")
    S.wait("gpsimd", sgb, 16)
    S.run("gpsimd", lambda e: e.collective_compute(
        "AllGather", OP.bypass, replica_groups=[list(range(NCORE))],
        ins=[g_bounce_in.ap()], outs=[g_bounce_out.ap()]), inc=scc)

    gRing.__exit__(None, None, None)
    gWG1.__exit__(None, None, None)
    gXT.__exit__(None, None, None)

    gSsb, Ssb = sb_open("S", [128, H * 128], F32, side="left")
    gGT, gtmp = zip(*[sb_open(f"gtmp{i}", [128, H * 128], F32, side="right")
                      for i in range(2)])
    S.wait("sync", scc, 1)
    S.wait("sync", t_ev, ring_cons_ev[2 * KC - 1])
    sin_step_ev = [0] * NCH
    for j in range(2):
        S.dma("sync", gtmp[j][:], g_bounce_out.ap()[j * 128:(j + 1) * 128, :], sga[j % 4])
    for j in range(NCH):
        S.wait("vector", sga[j % 4], 16 * (j // 4 + 1))
        for h in range(H):
            if j == 0:
                S.run("vector", lambda e, h=h: e.tensor_scalar(
                    Ssb[:, h * 128:(h + 1) * 128], gtmp[0][:, h * 128:(h + 1) * 128],
                    wcE[:, h:h + 1], None, OP.mult))
            else:
                S.run("vector", lambda e, j=j, h=h: e.scalar_tensor_tensor(
                    out=Ssb[:, h * 128:(h + 1) * 128],
                    in0=gtmp[j % 2][:, h * 128:(h + 1) * 128],
                    scalar=wcE[:, j * H + h:j * H + h + 1],
                    in1=Ssb[:, h * 128:(h + 1) * 128], op0=OP.mult, op1=OP.add))
        sin_step_ev[j] = S.mark("vector", v_ev)
        if j + 2 < NCH:
            S.wait("sync", v_ev, sin_step_ev[j])
            S.dma("sync", gtmp[(j + 2) % 2][:],
                  g_bounce_out.ap()[(j + 2) * 128:(j + 3) * 128, :], sga[(j + 2) % 4])
    sin_done_ev = sin_step_ev[NCH - 1]
    if "S_in" in tap_dram:
        tap("S_in", Ssb[:])
    for g in reversed(gGT):
        g.__exit__(None, None, None)

    maybe_stop("p3pro")
    gONT, o_nT = sb_open("o_nT", [128, KC * T], BF16, side="right")
    gDT, DT = sb_open("DT", [128, H * 128], BF16, side="right")
    gSBf, S_bf = sb_open("S_bf", [128, H * 128], BF16, side="right")
    gOB, ob = sb_open("ob", [128, D], BF16, side="right")
    gGC, gate_c = sb_open("gate_c", [128, D], BF16, side="right")
    gST, st_sb = zip(*[sb_open(f"st{i}", [128, 128], BF16, side="right")
                       for i in range(2)])
    gOBI, obi = sb_open("obi", [128, 128], F32, side="right")
    gSCR2, sq2k = sb_open("sq2k", [128, D], F32, side="right")
    gMS, msq = sb_open("msq", [128, 2 * H], F32, side="right")

    S.wait("scalar", v_ev, sin_done_ev)
    for h in range(H):
        sv = S.run("scalar", lambda e, h=h: e.activation(
            DT[:, h * 128:(h + 1) * 128], diffM[:], AF.Exp, scale=ld_b[:, h:h + 1]),
            inc=s_ev)
        S.wait("vector", s_ev, sv)
        S.run("vector", lambda e, h=h: e.tensor_tensor(
            DT[:, h * 128:(h + 1) * 128], DT[:, h * 128:(h + 1) * 128], tri[:], OP.mult))

    S.wait("tensor", s_ev, pp_free_ev[0])
    S.wait("tensor", s_ev, pp_free_ev[1])
    S.wait("tensor", v_ev, g_ready)

    def oi_slice(h):
        return pA[:, (h % 2) * 512:(h % 2) * 512 + 128]

    def oe_slice(h):
        return pA[:, 1024 + (h % 2) * 512:1024 + (h % 2) * 512 + 128]

    y_res_ev = [0] * NCH
    onT_prev = g_ready
    gating_prev = None
    mask_prev = None
    for c in range(NCH):
        for fb in range(4):
            pi = fb % 2
            if mask_prev is not None:
                S.wait("tensor", v_ev, mask_prev)
            if pp_free_ev[pi] is not None:
                S.wait("tensor", s_ev, pp_free_ev[pi])
            S.run("tensor", lambda e, pi=pi, c=c, fb=fb: e.matmul(
                pp[pi][:], g1T[:, c * 128:(c + 1) * 128],
                wg2_sb[:, fb * 512:(fb + 1) * 512], start=True, stop=True))
            tt = S.mark("tensor", t_ev)
            S.wait("scalar", t_ev, tt)
            S.wait("scalar", v_ev, sin_done_ev)
            if gating_prev is not None:
                S.wait("scalar", v_ev, gating_prev)
            pp_free_ev[pi] = S.run("scalar", lambda e, pi=pi, fb=fb: e.activation(
                gate_c[:, fb * 512:(fb + 1) * 512], pp[pi][:], AF.Sigmoid), inc=s_ev)
        gate_done = pp_free_ev[1]
        S.wait("vector", v_ev, sin_done_ev)
        S.run("vector", lambda e: e.tensor_copy(S_bf[:], Ssb[:]))
        sbf_ev = S.mark("vector", v_ev)
        S.wait("tensor", v_ev, onT_prev)
        for h in range(H):
            S.run("tensor", lambda e, h=h, c=c: e.transpose(
                pT2[:, h * 128:(h + 1) * 128],
                kT[:, h * T + c * 128:h * T + (c + 1) * 128], ident[:]))
        tt = S.mark("tensor", t_ev)
        S.wait("vector", t_ev, tt)
        for h in range(H):
            S.run("vector", lambda e, h=h: e.tensor_scalar(
                ktil[:, h * 128:(h + 1) * 128], pT2[:, h * 128:(h + 1) * 128],
                kdecT[:, h:h + 1], None, OP.mult))
        ktil_ev = S.mark("vector", v_ev)

        mask_ev = [0] * H
        comb_ev = [0] * H
        obi_ev = [0] * H
        S.wait("tensor", s_ev, pp_free_ev[0])
        S.wait("tensor", s_ev, pp_free_ev[1])
        for h in range(H):
            if h >= 2:
                S.wait("tensor", v_ev, mask_ev[h - 2])
            S.run("tensor", lambda e, h=h, c=c: e.matmul(
                pp[h % 2][:, 0:128], kT[:, h * T + c * 128:h * T + (c + 1) * 128],
                qT[:, h * T + c * 128:h * T + (c + 1) * 128], start=True, stop=True))
            sc = S.mark("tensor", t_ev)
            S.wait("vector", t_ev, sc)
            mask_ev[h] = S.run("vector", lambda e, h=h: e.tensor_tensor(
                st_sb[h % 2][:], pp[h % 2][:, 0:128], DT[:, h * 128:(h + 1) * 128],
                OP.mult), inc=v_ev)
            S.wait("tensor", v_ev, mask_ev[h])
            if h >= 2:
                S.wait("tensor", v_ev, obi_ev[h - 2])
                S.wait("tensor", v_ev, comb_ev[h - 2])
            S.run("tensor", lambda e, h=h, c=c: e.matmul(
                oi_slice(h), st_sb[h % 2][:],
                v_sb[:, c * D + h * 128:c * D + (h + 1) * 128], start=True, stop=True))
            if h < 2:
                S.wait("tensor", v_ev, sbf_ev)
            S.run("tensor", lambda e, h=h, c=c: e.matmul(
                oe_slice(h), qT[:, h * T + c * 128:h * T + (c + 1) * 128],
                S_bf[:, h * 128:(h + 1) * 128], start=True, stop=True))
            tt = S.mark("tensor", t_ev)
            S.wait("vector", t_ev, tt)
            obi_ev[h] = S.run("vector", lambda e, h=h: e.tensor_copy(obi[:], oi_slice(h)),
                              inc=v_ev)
            comb_ev[h] = S.run("vector", lambda e, h=h: e.scalar_tensor_tensor(
                out=ob[:, h * 128:(h + 1) * 128], in0=oe_slice(h),
                scalar=qdecT[:, h:h + 1], in1=obi[:], op0=OP.mult, op1=OP.add), inc=v_ev)
        mask_prev = mask_ev[H - 1]

        if _os.environ.get("P3CUT") == "comb" and c + 1 >= int(_os.environ.get("P3N", "1")):
            if "ob" in tap_dram:
                tap("ob", ob[:])
            if "st0" in tap_dram:
                tap("st0", st_sb[0][:])
            if "st1" in tap_dram:
                tap("st1", st_sb[1][:])
            if "obi" in tap_dram:
                tap("obi", obi[:])
            if "sbf" in tap_dram:
                tap("sbf", S_bf[:])
            S.dead = True
        S.wait("tensor", v_ev, comb_ev[H - 1])
        S.wait("tensor", v_ev, ktil_ev)
        for h in range(H):
            S.run("tensor", lambda e, h=h, c=c: e.matmul(
                pA[:, h * 128:(h + 1) * 128], ktil[:, h * 128:(h + 1) * 128],
                v_sb[:, c * D + h * 128:c * D + (h + 1) * 128], start=True, stop=True))
        tt_a = S.mark("tensor", t_ev)

        S.wait("vector", s_ev, gate_done)
        gating_prev = S.run("vector", lambda e, c=c: e.tensor_tensor(
            ob[:], ob[:], gate_c[:], OP.mult), inc=v_ev)
        S.wait("scalar", v_ev, gating_prev)
        gn_sq = S.run("scalar", lambda e: e.activation(sq2k[:], ob[:], AF.Square),
                      inc=s_ev)
        S.wait("vector", s_ev, gn_sq)
        vstep(lambda e: e.reduce_sum(
            msq[:, 0:H], sq2k[:].rearrange("p (h e) -> p h e", h=H), axis=AX.X))
        vv = vstep(lambda e: e.tensor_scalar(
            msq[:, 0:H], msq[:, 0:H], 1.0 / 128, EPS, OP.mult, OP.add))
        S.wait("scalar", v_ev, vv)
        sv = S.run("scalar", lambda e: e.activation(
            msq[:, H:2 * H], msq[:, 0:H], AF.Sqrt), inc=s_ev)
        S.wait("vector", s_ev, sv)
        vstep(lambda e: e.reciprocal(msq[:, H:2 * H], msq[:, H:2 * H]))
        for h in range(H):
            S.run("vector", lambda e, h=h: e.tensor_scalar(
                ob[:, h * 128:(h + 1) * 128], ob[:, h * 128:(h + 1) * 128],
                msq[:, H + h:H + h + 1], None, OP.mult))
        ob_ev = S.mark("vector", v_ev)

        S.wait("vector", t_ev, tt_a)
        for h in range(H):
            S.run("vector", lambda e, h=h: e.scalar_tensor_tensor(
                out=Ssb[:, h * 128:(h + 1) * 128], in0=Ssb[:, h * 128:(h + 1) * 128],
                scalar=gchunk[:, h:h + 1], in1=pA[:, h * 128:(h + 1) * 128],
                op0=OP.mult, op1=OP.add))
        supd_prev = S.mark("vector", v_ev)

        S.wait("tensor", v_ev, ob_ev)
        for ft in range(KC):
            S.run("tensor", lambda e, ft=ft: e.transpose(
                pT2[:, ft * 128:(ft + 1) * 128], ob[:, ft * 128:(ft + 1) * 128], ident[:]))
        tt = S.mark("tensor", t_ev)
        S.wait("vector", t_ev, tt)
        for ft in range(KC):
            S.run("vector", lambda e, ft=ft, c=c: e.tensor_scalar(
                o_nT[:, ft * T + c * 128:ft * T + (c + 1) * 128],
                pT2[:, ft * 128:(ft + 1) * 128], nw_sb[:, ft:ft + 1], None, OP.mult))
        onT_prev = S.mark("vector", v_ev)
        y_res_ev[c] = onT_prev
        if _os.environ.get("P3CUT") == "chunk" and c + 1 >= int(_os.environ.get("P3N", "1")):
            if "ob" in tap_dram:
                tap("ob", ob[:])
            S.dead = True

    if "S_fin" in tap_dram:
        tap("S_fin", Ssb[:])
    if "onT" in tap_dram:
        tap("onT", o_nT[:])

    maybe_stop("p3")
    p3_done = y_res_ev[NCH - 1]

    for g in [gMS, gSCR2, gOBI] + list(reversed(gST)) + [gGC, gOB, gSBf, gDT]:
        g.__exit__(None, None, None)

    gSsb.__exit__(None, None, None)
    gG1.__exit__(None, None, None)
    gQT.__exit__(None, None, None)
    gGsb.__exit__(None, None, None)
    gKtil.__exit__(None, None, None)
    gKT.__exit__(None, None, None)
    gV.__exit__(None, None, None)

    gY, y_acc = sb_open("y_acc", [128, NCH * D], F32, side="left")
    gWO, wo_sb = sb_open("wo_sb", [128, KC * D], BF16, side="right")
    gXB2, xb2 = sb_open("xb2r", [128, D], F32, side="right")

    gPA.__exit__(None, None, None)
    gPT2.__exit__(None, None, None)
    gPP1.__exit__(None, None, None)
    gPP0.__exit__(None, None, None)
    gPY0, pY0 = ps_open("pY0", [128, D], F32)
    gPY1, pY1 = ps_open("pY1", [128, D], F32)
    pY = [pY0, pY1]

    S.wait("gpsimd", v_ev, p3_done)
    for kc in range(KC):
        S.dma("gpsimd", wo_sb[:, kc * D:(kc + 1) * D],
              wo_in[kc * 128:(kc + 1) * 128, :], swp[kc // 2])

    res_ev = [0] * NCH
    S.wait("sync", v_ev, p3_done)
    S.dma("sync", xb2[:], x_in[0:128, :], sx[0])
    for tc in range(NCH):
        pi = tc % 2
        S.wait("tensor", v_ev, p3_done)
        if tc >= 2:
            S.wait("tensor", v_ev, res_ev[tc - 2])
        for kc in range(KC):
            S.wait("tensor", swp[kc // 2], 32 + 16 * (kc % 2 + 1))
            for fb in range(4):
                S.run("tensor", lambda e, pi=pi, tc=tc, kc=kc, fb=fb: e.matmul(
                    pY[pi][:, fb * 512:(fb + 1) * 512],
                    o_nT[:, kc * T + tc * 128:kc * T + (tc + 1) * 128],
                    wo_sb[:, kc * D + fb * 512:kc * D + (fb + 1) * 512],
                    start=(kc == 0), stop=(kc == KC - 1)))
        tt = S.mark("tensor", t_ev)
        S.wait("vector", t_ev, tt)
        S.wait("vector", sx[tc], 32)
        res_ev[tc] = S.run("vector", lambda e, pi=pi, tc=tc: e.tensor_tensor(
            y_acc[:, tc * D:(tc + 1) * D], pY[pi][:], xb2[:], OP.add), inc=v_ev)
        if tc + 1 < NCH:
            S.wait("sync", v_ev, res_ev[tc])
            S.dma("sync", xb2[:], x_in[(tc + 1) * 128:(tc + 2) * 128, :], sx[tc + 1])

    if "x_mid" in tap_dram:
        tap("x_mid", y_acc[:])

    wo_last_mm = S.cnt[t_ev]
    gXB2.__exit__(None, None, None)
    gWO.__exit__(None, None, None)
    gONT.__exit__(None, None, None)
    gWG2.__exit__(None, None, None)

    maybe_stop("p3b")
    gX2T, xhat2T = sb_open("xhat2T", [128, KC * T], BF16, side="left")
    gHT2, xh2tm = zip(*[sb_open(f"xh2tm{i}", [128, D], BF16, side="right")
                        for i in range(2)])
    gSCR3, scr2k2 = sb_open("scr2k2", [128, D], F32, side="right")

    gPY1.__exit__(None, None, None)
    gPT3, pT3 = ps_open("pT3", [128, D], BF16)
    pt3_3d = pT3[:].rearrange("p (kc t) -> p kc t", kc=KC)

    x2T_ev = [0] * NCH
    tr2_ev = [0] * NCH
    for c in range(NCH):
        S.wait("vector", v_ev, res_ev[c])
        S.wait("scalar", v_ev, res_ev[NCH - 1] if c == 0 else res_ev[c])
        if c == 0:
            S.wait("scalar", t_ev, wo_last_mm)
        stat_ev = ln_stats(c, y_acc[:, c * D:(c + 1) * D], scr2k2)
        S.wait("scalar", v_ev, stat_ev)
        if c >= 2:
            S.wait("scalar", t_ev, tr2_ev[c - 2])
        nev = S.run("scalar", lambda e, c=c: e.activation(
            xh2tm[c % 2][:], y_acc[:, c * D:(c + 1) * D], AF.Identity,
            scale=stats[:, RS0 + c:RS0 + c + 1], bias=stats[:, NB0 + c:NB0 + c + 1]),
            inc=s_ev)
        S.wait("tensor", s_ev, nev)
        if c == 0:
            S.wait("tensor", v_ev, res_ev[NCH - 1])
        else:
            S.wait("tensor", v_ev, x2T_ev[c - 1])
        for ft in range(KC):
            S.run("tensor", lambda e, ft=ft, c=c: e.transpose(
                pT3[:, ft * 128:(ft + 1) * 128], xh2tm[c % 2][:, ft * 128:(ft + 1) * 128],
                ident[:]))
        tr2_ev[c] = S.mark("tensor", t_ev)
        S.wait("vector", t_ev, tr2_ev[c])
        x2T_ev[c] = S.run("vector", lambda e, c=c: e.tensor_copy(
            kcslice(xhat2T, c), pt3_3d), inc=v_ev)

    ln2_done_t = tr2_ev[NCH - 1]
    ln2_done_v = x2T_ev[NCH - 1]
    gSCR3.__exit__(None, None, None)
    for g in reversed(gHT2):
        g.__exit__(None, None, None)

    if "xhat2T" in tap_dram:
        tap("xhat2T", xhat2T[:])

    maybe_stop("p3c")
    FDS = 6
    YB = 4
    gFG, fg_ring = sb_open("fg_ring", [128, RW * D], BF16, side="right")
    gFU, fu_ring = sb_open("fu_ring", [128, RW * D], BF16, side="right")
    gFD, fd_ring = sb_open("fd_ring", [128, FDS * D], BF16, side="right")
    gHS, hs_ring = sb_open("hs_ring", [128, NSLOT * T], BF16, side="right")
    gHQ, hq = sb_open("hq", [128, 4 * T], BF16, side="right")
    gAA, a_bf = sb_open("a_bf", [128, 4 * 512], BF16, side="right")
    gEE, e_bf = sb_open("e_bf", [128, 6 * 512], BF16, side="right")
    gTT2, tmp_bf = sb_open("tmp_bf", [128, 2 * 512], BF16, side="right")

    gPT3.__exit__(None, None, None)
    gPH = []
    phg = []
    phu = []
    for i in range(2):
        g, p = ps_open(f"phg{i}", [128, 512], F32); gPH.append(g); phg.append(p)
    for i in range(2):
        g, p = ps_open(f"phu{i}", [128, 512], F32); gPH.append(g); phu.append(p)

    fgu_cons_ev = {}
    fd_cons_ev = {}
    hs_ev = {}
    hq_free_ev = None
    phg_free = [None, None]
    phu_free = [None, None]

    def fgu_fill(slab):
        slot = slab % RW
        if slab < RW:
            S.wait("gpsimd", t_ev, ln2_done_t)
            S.wait("gpsimd", v_ev, ln2_done_v)
        else:
            S.wait("gpsimd", t_ev, fgu_cons_ev[slab - RW])
        S.dma("gpsimd", fg_ring[:, slot * D:(slot + 1) * D],
              fg_in[slab * 128:(slab + 1) * 128, :], sfg[slot])
        S.dma("gpsimd", fu_ring[:, slot * D:(slot + 1) * D],
              fu_in[slab * 128:(slab + 1) * 128, :], sfu[slot])

    def fd_fill(slab):
        slot = slab % FDS
        if slab < FDS:
            S.wait("gpsimd", t_ev, ln2_done_t)
            S.wait("gpsimd", v_ev, ln2_done_v)
        else:
            S.wait("gpsimd", t_ev, fd_cons_ev[slab - FDS])
        S.dma("gpsimd", fd_ring[:, slot * D:(slot + 1) * D],
              fd_in[slab * 128:(slab + 1) * 128, :], sfd[slot])

    yps_ev = [None, None]
    last_y_ev = [0] * NCH
    fd_next = [FDS]
    cmp_done = [0]

    def emit_yblock(bi):
        block = list(range(4 * bi, 4 * bi + 4))
        for tc in range(NCH):
            ev = None
            for fh in range(2):
                if yps_ev[fh] is None:
                    S.wait("tensor", v_ev, res_ev[NCH - 1])
                else:
                    S.wait("tensor", v_ev, yps_ev[fh])
                for n, ic in enumerate(block):
                    S.wait("tensor", v_ev, hs_ev[ic])
                    S.wait("tensor", sfd[ic % FDS], 16 * (ic // FDS + 1))
                    for sub in range(2):
                        S.run("tensor", lambda e, ic=ic, tc=tc, fh=fh, sub=sub, n=n,
                              last=(len(block) - 1): e.matmul(
                            pY0[:, fh * 1024 + sub * 512:fh * 1024 + (sub + 1) * 512],
                            hs_ring[:, (ic % NSLOT) * T + tc * 128:
                                    (ic % NSLOT) * T + (tc + 1) * 128],
                            fd_ring[:, (ic % FDS) * D + fh * 1024 + sub * 512:
                                    (ic % FDS) * D + fh * 1024 + (sub + 1) * 512],
                            start=(n == 0), stop=(n == last)))
                tt = S.mark("tensor", t_ev)
                S.wait("vector", t_ev, tt)
                ev = S.run("vector", lambda e, tc=tc, fh=fh: e.tensor_tensor(
                    y_acc[:, tc * D + fh * 1024:tc * D + (fh + 1) * 1024],
                    y_acc[:, tc * D + fh * 1024:tc * D + (fh + 1) * 1024],
                    pY0[:, fh * 1024:(fh + 1) * 1024], OP.add), inc=v_ev)
                yps_ev[fh] = ev
            last_y_ev[tc] = ev
        tt = S.mark("tensor", t_ev)
        for ic in block:
            fd_cons_ev[ic] = tt
        hi = min(block[-1] + FDS, IT - 1)
        while fd_next[0] <= hi:
            fd_fill(fd_next[0])
            fd_next[0] += 1

    _p4cut = _os.environ.get("P4CUT")
    for slab in range(RW):
        fgu_fill(slab)
    for slab in range(FDS):
        fd_fill(slab)
    if _p4cut == "fills":
        S.dead = True

    EIDX = {(0, 1): 0, (0, 2): 1, (0, 3): 2, (1, 2): 3, (1, 3): 4, (2, 3): 5}

    for q in range(16):
        for j in range(4):
            slab = 4 * q + j
            slot = slab % RW
            S.wait("tensor", sfg[slot], 16 * (slab // RW + 1))
            S.wait("tensor", sfu[slot], 16 * (slab // RW + 1))
            if q == 0 and j == 0:
                S.wait("tensor", v_ev, ln2_done_v)
            for half in range(2):
                S.wait("tensor", v_ev, x2T_ev[4 * half + 3])
                pgi = half
                if phg_free[pgi] is not None:
                    S.wait("tensor", s_ev, phg_free[pgi])
                for kc in range(KC):
                    S.run("tensor", lambda e, pgi=pgi, slot=slot, kc=kc, half=half:
                          e.matmul(
                        phg[pgi][:], fg_ring[:, slot * D + kc * 128:slot * D + (kc + 1) * 128],
                        xhat2T[:, kc * T + half * 512:kc * T + (half + 1) * 512],
                        start=(kc == 0), stop=(kc == KC - 1)))
                tthg = S.mark("tensor", t_ev)
                if phu_free[pgi] is not None:
                    S.wait("tensor", v_ev, phu_free[pgi])
                for kc in range(KC):
                    S.run("tensor", lambda e, pgi=pgi, slot=slot, kc=kc, half=half:
                          e.matmul(
                        phu[pgi][:], fu_ring[:, slot * D + kc * 128:slot * D + (kc + 1) * 128],
                        xhat2T[:, kc * T + half * 512:kc * T + (half + 1) * 512],
                        start=(kc == 0), stop=(kc == KC - 1)))
                tthu = S.mark("tensor", t_ev)
                if half == 1:
                    fgu_cons_ev[slab] = tthu
                    if slab + RW < IT:
                        fgu_fill(slab + RW)
                S.wait("scalar", t_ev, tthg)
                if hq_free_ev is not None:
                    S.wait("scalar", v_ev, hq_free_ev)
                else:
                    S.wait("scalar", v_ev, ln2_done_v)
                phg_free[pgi] = S.run("scalar", lambda e, pgi=pgi, j=j, half=half:
                                      e.activation(
                    hq[:, j * T + half * 512:j * T + (half + 1) * 512], phg[pgi][:],
                    AF.Silu), inc=s_ev)
                S.wait("vector", s_ev, phg_free[pgi])
                S.wait("vector", t_ev, tthu)
                phu_free[pgi] = S.run("vector", lambda e, pgi=pgi, j=j, half=half:
                                      e.tensor_tensor(
                    hq[:, j * T + half * 512:j * T + (half + 1) * 512],
                    hq[:, j * T + half * 512:j * T + (half + 1) * 512],
                    phu[pgi][:], OP.mult), inc=v_ev)
            if _p4cut == f"mul{j}":
                if "hq" in tap_dram:
                    tap("hq", hq[:])
                if "fg0" in tap_dram:
                    tap("fg0", fg_ring[:])
                if "x2" in tap_dram:
                    tap("x2", xhat2T[:])
                S.dead = True

        for half in range(2):
            S.wait("scalar", v_ev, phu_free[1])
            S.wait("scalar", v_ev, phu_free[0])
            S.wait("scalar", v_ev, cmp_done[0])
            abs_evs = []
            for j in range(4):
                abs_evs.append(S.run("scalar", lambda e, j=j, half=half: e.activation(
                    a_bf[:, j * 512:(j + 1) * 512],
                    hq[:, j * T + half * 512:j * T + (half + 1) * 512], AF.Abs),
                    inc=s_ev))
            S.wait("vector", s_ev, abs_evs[-1])
            if _p4cut == "abs":
                S.dead = True
            for (i0, i1), n in EIDX.items():
                S.run("vector", lambda e, n=n, i0=i0, i1=i1: e.tensor_tensor(
                    e_bf[:, n * 512:(n + 1) * 512], a_bf[:, i0 * 512:(i0 + 1) * 512],
                    a_bf[:, i1 * 512:(i1 + 1) * 512], OP.is_ge))
            cmp_done[0] = S.mark("vector", v_ev)
            if _p4cut == "cmp":
                S.dead = True

            def EE(n):
                return e_bf[:, n * 512:(n + 1) * 512]

            keep_defs = [
                ([(1, 0), (1, 1), (1, 2)], 1.5, OP.is_ge),
                ([(1, 3), (1, 4), (-1, 0)], 0.5, OP.is_ge),
                ([(1, 5), (-1, 1), (-1, 3)], -0.5, OP.is_ge),
                ([(1, 2), (1, 4), (1, 5)], 1.5, OP.is_le),
            ]
            for j, (terms, th, op) in enumerate(keep_defs):
                dst = tmp_bf[:, 0:512] if j % 2 == 0 else tmp_bf[:, 512:1024]
                first = True
                for sign, idx in terms:
                    if first:
                        if sign > 0:
                            S.run("vector", lambda e, idx=idx, dst=dst: e.tensor_copy(
                                dst, EE(idx)))
                        else:
                            S.run("vector", lambda e, idx=idx, dst=dst: e.tensor_scalar(
                                dst, EE(idx), -1.0, None, OP.mult))
                        first = False
                    else:
                        S.run("vector", lambda e, idx=idx, dst=dst, sign=sign:
                              e.tensor_tensor(dst, dst, EE(idx),
                                              OP.add if sign > 0 else OP.subtract))
                S.run("vector", lambda e, dst=dst, th=th, op=op: e.tensor_single_scalar(
                    dst, dst, th, op))
                slab = 4 * q + j
                slot = slab % NSLOT
                if slab >= NSLOT:
                    S.wait("vector", t_ev, fd_cons_ev[slab - NSLOT])
                ev = S.run("vector", lambda e, slot=slot, half=half, j=j, dst=dst:
                           e.tensor_tensor(
                    hs_ring[:, slot * T + half * 512:slot * T + (half + 1) * 512],
                    hq[:, j * T + half * 512:j * T + (half + 1) * 512], dst, OP.mult),
                    inc=v_ev)
                if half == 1:
                    hs_ev[slab] = ev
            hq_free_ev = S.mark("vector", v_ev)

        if _os.environ.get("P4Q") and q + 1 >= int(_os.environ["P4Q"]):
            if "hs" in tap_dram:
                tap("hs", hs_ring[:])
            if "hq" in tap_dram:
                tap("hq", hq[:])
            S.dead = True
        emit_yblock(q)

    for tc in range(NCH):
        S.wait("sync", v_ev, last_y_ev[tc])
        S.dma("sync", out_t.ap()[tc * 128:(tc + 1) * 128, :],
              y_acc[:, tc * D:(tc + 1) * D], sst)
    S.wait("sync", sst, 16 * NCH)
    if tap_dram:
        S.wait("sync", stap, S.cnt[stap])

    if S.dead:
        S.dead = False
        S.dma("sync", out_t.ap()[0:128, 0:96], stats[:], sst)
        S.wait("sync", sst, S.cnt[sst])
        if tap_dram:
            S.wait("sync", stap, S.cnt[stap])

    import os as _os
    _vt = _os.environ.get("VEC_TRUNC")
    if _vt is not None:
        S.ops["vector"] = S.ops["vector"][:int(_vt)]
    _st_ = _os.environ.get("SCA_TRUNC")
    if _st_ is not None:
        S.ops["scalar"] = S.ops["scalar"][:int(_st_)]
    with nc.Block() as block:
        @block.sync
        def _(e):
            for _k, f, _r in S.ops["sync"]:
                f(e)

        @block.gpsimd
        def _(e):
            for _k, f, _r in S.ops["gpsimd"]:
                f(e)

        @block.tensor
        def _(e):
            for _k, f, _r in S.ops["tensor"]:
                f(e)

        @block.vector
        def _(e):
            for _k, f, _r in S.ops["vector"]:
                f(e)

        @block.scalar
        def _(e):
            for _k, f, _r in S.ops["scalar"]:
                f(e)

    return nc


def _stage(inputs):
    x = np.ascontiguousarray(np.asarray(inputs["L0_attn_x"], dtype=np.float32).reshape(8192, D))
    ld = np.ascontiguousarray(np.broadcast_to(
        np.asarray(inputs["L0_attn_log_decay"], np.float32).reshape(1, H), (128, H)))
    wq = np.asarray(inputs["L0_attn_Wq"], np.float32)
    wk = np.asarray(inputs["L0_attn_Wk"], np.float32)
    wv = np.ascontiguousarray(np.asarray(inputs["L0_attn_Wv"], np.float32))
    wo = np.ascontiguousarray(np.asarray(inputs["L0_attn_Wo"], np.float32))
    wg1 = np.asarray(inputs["L0_attn_Wg1"], np.float32)
    wg2 = np.ascontiguousarray(np.asarray(inputs["L0_attn_Wg2"], np.float32))
    nw = np.ascontiguousarray(np.asarray(inputs["L0_attn_norm_w"], np.float32).reshape(H, 128).T)
    fg = np.asarray(inputs["L0_ffn_w_gate"], np.float32)
    fu = np.asarray(inputs["L0_ffn_w_up"], np.float32)
    fd = np.asarray(inputs["L0_ffn_w_down"], np.float32)

    def ft_slabs(w):
        nft = w.shape[1] // 128
        return np.ascontiguousarray(
            w.reshape(KC, 128, nft, 128).transpose(2, 1, 0, 3).reshape(nft * 128, KC * 128))

    wq_s = ft_slabs(wq)
    wk_s = ft_slabs(wk)
    wg1_s = np.ascontiguousarray(wg1.reshape(KC, 128, 128).transpose(1, 0, 2).reshape(128, D))

    tprime = np.arange(IT)
    p = np.arange(128)
    perm = (4 * ((tprime[:, None] // 4) * 128 + p[None, :]) + (tprime[:, None] % 4)).reshape(-1)
    fg_s = ft_slabs(fg[:, perm])
    fu_s = ft_slabs(fu[:, perm])
    fd_p = np.ascontiguousarray(fd[perm, :])

    p = np.arange(128, dtype=np.float32)
    diffc = np.arange(128, dtype=np.float32)[None, :] - p[:, None]
    tric = (diffc >= 0).astype(np.float32)
    consts = np.zeros((128, 386), np.float32)
    consts[:, 0] = p + 1
    consts[:, 1] = 127 - p
    consts[:, 2:130] = diffc * tric
    consts[:, 130:258] = tric
    consts[:, 258:386] = np.eye(128, dtype=np.float32)
    consts = np.ascontiguousarray(consts)

    in_maps = []
    for c in range(NCORE):
        ck = np.full((128, NCH), 4e9, np.float32)
        for j in range(c):
            ck[:, j] = 1024.0 * (c - 1 - j)
        in_maps.append({
            "x": np.ascontiguousarray(x[c * T:(c + 1) * T]),
            "wq": wq_s, "wk": wk_s, "wv": wv, "wo": wo,
            "wg1": wg1_s, "wg2": wg2, "ldec": ld, "nw": nw, "ck": ck, "consts": consts,
            "fg": fg_s, "fu": fu_s, "fd": fd_p,
        })
    return in_maps


_NC_CACHE = {}


def kernel(**inputs):
    key = tuple(sorted(_DEBUG_TAPS))
    if key not in _NC_CACHE:
        _NC_CACHE[key] = build_program(taps=_DEBUG_TAPS)
    nc = _NC_CACHE[key]
    in_maps = _stage(inputs)
    res = run_bass_kernel_spmd(nc, in_maps, core_ids=list(range(NCORE)))
    out = np.concatenate([np.asarray(r["out"]) for r in res.results], axis=0)
    if _DEBUG_TAPS:
        kernel.tap_results = [{k: np.asarray(v) for k, v in r.items() if k.startswith("tap_")}
                              for r in res.results]
    return out.reshape(1, 8192, D).astype(np.float32)
